# version 06
# baseline (speedup 1.0000x reference)
import numpy as np
from contextlib import ExitStack

import concourse.bass as bass
import concourse.mybir as mybir
from concourse.bass_utils import run_bass_kernel_spmd

F32 = mybir.dt.float32
BF16 = mybir.dt.bfloat16
I32 = mybir.dt.int32
I16 = mybir.dt.int16
U16 = mybir.dt.uint16
U8 = mybir.dt.uint8
AF = mybir.ActivationFunctionType
OP = mybir.AluOpType
AX = mybir.AxisListType

B, S, E, D, II = 4, 2048, 2048, 2048, 5504
NE, ND, NI = 16, 16, 43
TPC, NT = 1024, 8
SCALE = 1.0 / float(np.sqrt(2048.0))
EPS = 1e-5
KT = [2 * j + 2 for j in range(NT)]
KC = [(kt * 128 + 511) // 512 for kt in KT]
KB = 1024
DBG = False
DBG_A = False
DBG_F = False
DBG_F2 = False
DBG_END = False


def _build():
    nc = bass.Bass("TRN2", target_bir_lowering=False, detect_race_conditions=False)

    xT = nc.dram_tensor("xT", [E, S], F32, kind="ExternalInput").ap()
    x_ownT = nc.dram_tensor("x_ownT", [E, TPC], F32, kind="ExternalInput").ap()
    wq = nc.dram_tensor("wq", [D, E], F32, kind="ExternalInput").ap()
    wk = nc.dram_tensor("wk", [D, E], F32, kind="ExternalInput").ap()
    wv = nc.dram_tensor("wv", [D, E], F32, kind="ExternalInput").ap()
    gidx = nc.dram_tensor("gidx", [II, E], I32, kind="ExternalInput").ap()
    uidx = nc.dram_tensor("uidx", [II, E], I32, kind="ExternalInput").ap()
    didx = nc.dram_tensor("didx", [E, II], I32, kind="ExternalInput").ap()
    lut128 = nc.dram_tensor("lut128", [128, 16], F32, kind="ExternalInput").ap()
    ones_in = nc.dram_tensor("ones_in", [128, 128], F32, kind="ExternalInput").ap()
    id_in = nc.dram_tensor("id_in", [128, 128], F32, kind="ExternalInput").ap()
    band_in = nc.dram_tensor("band_in", [128, 2560], F32, kind="ExternalInput").ap()
    owncols = nc.dram_tensor("owncols", [128, 32], U16, kind="ExternalInput").ap()
    out = nc.dram_tensor("out", [E, TPC], F32, kind="ExternalOutput").ap()

    wq_bf = nc.dram_tensor("wq_bf", [D, E], BF16, kind="Internal").ap()
    wk_bf = nc.dram_tensor("wk_bf", [D, E], BF16, kind="Internal").ap()
    wv_bf = nc.dram_tensor("wv_bf", [D, E], BF16, kind="Internal").ap()
    w_g = nc.dram_tensor("w_g", [II, E], BF16, kind="Internal").ap()
    w_u = nc.dram_tensor("w_u", [II, E], BF16, kind="Internal").ap()
    w_d = nc.dram_tensor("w_d", [E, II], BF16, kind="Internal").ap()
    vfm = nc.dram_tensor("vfm", [D, S], BF16, kind="Internal").ap()
    attn_mid = nc.dram_tensor("attn_mid", [NT, D, 128], F32, kind="Internal").ap()
    if DBG or DBG_A or DBG_F or DBG_F2 or DBG_END:
        dbg_xn = nc.dram_tensor("dbg_xn", [E, S], BF16, kind="Internal").ap()
        dbg_xo = nc.dram_tensor("dbg_xo", [E, TPC], BF16, kind="Internal").ap()
        dbg_q = nc.dram_tensor("dbg_q", [D, TPC], BF16, kind="Internal").ap()
        dbg_k = nc.dram_tensor("dbg_k", [D, S], BF16, kind="Internal").ap()
        dbg_s = nc.dram_tensor("dbg_s", [NT, 128, 2048], F32, kind="Internal").ap()
        dbg_p = nc.dram_tensor("dbg_p", [NT, 128, 2048], BF16, kind="Internal").ap()
        dbg_h = nc.dram_tensor("dbg_h", [II, TPC], BF16, kind="Internal").ap()
        dbg_g = nc.dram_tensor("dbg_g", [II, TPC], F32, kind="Internal").ap()
        dbg_u = nc.dram_tensor("dbg_u", [II, TPC], F32, kind="Internal").ap()


    es = ExitStack()
    arena = es.enter_context(nc.sbuf_tensor("arena", [128, 168 * KB], U8)).ap()

    def V(off_b, dtype, shape):
        n = int(np.prod(shape)) * mybir.dt.size(dtype)
        a = arena[:, off_b: off_b + n].bitcast(dtype)
        if len(shape) > 1:
            names = [chr(ord('a') + i) for i in range(len(shape))]
            a = a.rearrange("p (" + " ".join(names) + ") -> p " + " ".join(names),
                            **dict(zip(names, shape)))
        return a

    band_sb = V(0, BF16, [2560])
    lut_sb = V(5120, F32, [16])
    lutd_sb = V(5184, F32, [8])
    ones_sb = V(5248, BF16, [128])
    idb_sb = V(5504, BF16, [128])
    idf_sb = V(5760, F32, [128])
    oc_sb = V(6272, U16, [32])
    nmax = V(6400, F32, [1])
    rsum = V(6432, F32, [1])
    rrec = V(6464, F32, [1])
    A1 = 8 * KB
    xn = V(A1, BF16, [NE, S])
    q_sb = V(A1, BF16, [ND, TPC])
    vpan = V(40 * KB, BF16, [2, 2048])
    s_sb = V(48 * KB, F32, [2048])
    af_sb = V(48 * KB, F32, [2048])
    p_sb = V(56 * KB, BF16, [2048])
    pt_sb = V(60 * KB, BF16, [16, 128])
    at_sb = V(64 * KB, F32, [2048])
    gpan = V(A1, I16, [2, 2816])
    gslab = V(19 * KB, BF16, [2, 2816])
    h_sb = V(A1, BF16, [NI, TPC])
    k_sb = V(72 * KB, BF16, [ND, S])
    m_row = V(72 * KB, F32, [2048])
    q_row = V(80 * KB, F32, [2048])
    rb_row = V(88 * KB, F32, [2048])
    mr_row = V(96 * KB, F32, [2048])
    xld = V(104 * KB, F32, [2, 2048])
    xsq = V(120 * KB, BF16, [2, 2048])
    gt = V(72 * KB, BF16, [8, 2048])
    gm = V(104 * KB, I16, [3, 2048])
    gb0 = V(116 * KB, BF16, [2048])
    ffn_sets = V(96 * KB, BF16, [3, 16, 256])
    dn_sets = V(96 * KB, BF16, [2, 43, 256])
    xres = V(160 * KB, F32, [512])
    ares = V(162 * KB, F32, [4, 128])
    ores = V(164 * KB, F32, [2, 512])
    silu_sb = V(120 * KB, BF16, [2, 512])
    xn_own = V(136 * KB, BF16, [NE, TPC])
    kv_sets = V(136 * KB, BF16, [2, 16, 256])
    vev = V(152 * KB, BF16, [2, 2048])
    qset_sets = V(40 * KB, BF16, [2, 16, 256])

    pp = es.enter_context(nc.psum_tensor("pp", [128, 8, 512], F32)).ap()
    ptb = [pp[:, 4, 0:64].bitcast(BF16), pp[:, 5, 0:64].bitcast(BF16)]
    ptf = [pp[:, 6, 0:128], pp[:, 7, 0:128]]

    class Ctr:
        def __init__(self, s, step):
            self.s, self.step, self.n = s, step, 0

    C = {}
    for name in ("TE", "DV", "AC", "GC"):
        C[name] = Ctr(es.enter_context(nc.semaphore("s" + name)), 1)
    for name in ("C", "W", "X0", "X1", "KS0", "KS1", "QS0", "QS1", "VP0",
                 "VP1", "VE0", "VE1", "AM", "GP0", "GP1", "SL0", "SL1",
                 "FS0", "FS1", "FS2", "DS0", "DS1", "XR", "AR", "ST"):
        C[name] = Ctr(es.enter_context(nc.semaphore("d" + name)), 16)

    plan = []

    def P(eng, fn, waits=None, inc=None):
        w = []
        if waits:
            for x in waits:
                if x is None:
                    continue
                if isinstance(x, list):
                    w += [y for y in x if y is not None]
                else:
                    w.append(x)
        c = C[inc] if inc else None
        if c:
            c.n += c.step
        plan.append((eng, fn, w, c))
        return (c.s, c.n) if c else None

    def DMA(q, semname, fn, waits=None):
        return P(q, fn, waits=waits, inc=semname)

    def tok(name):
        return (C[name].s, C[name].n)

    DMA("gpsimd", "C", lambda g: g.dma_start(out=lut_sb, in_=lut128[:]))
    DMA("gpsimd", "C", lambda g: g.dma_start(out=ones_sb, in_=ones_in[:]))
    DMA("gpsimd", "C", lambda g: g.dma_start(out=idb_sb, in_=id_in[:]))
    DMA("gpsimd", "C", lambda g: g.dma_start(out=idf_sb, in_=id_in[:]))
    DMA("gpsimd", "C", lambda g: g.dma_start(out=band_sb, in_=band_in[:]))
    DMA("gpsimd", "C", lambda g: g.dma_start(out=oc_sb, in_=owncols[:]))
    consts_done = tok("C")
    for nm, dst, src in (("q", wq_bf, wq), ("k", wk_bf, wk), ("v", wv_bf, wv)):
        DMA("gpsimd", "W", lambda g, d=dst, s=src: g.dma_start(out=d[:], in_=s[:]))
    wcast_done = tok("W")

    for k in range(8):
        P("vector", lambda v, k=k: v.tensor_tensor(
            out=lutd_sb[:, k:k + 1], in0=lut_sb[:, 2 * k + 1:2 * k + 2],
            in1=lut_sb[:, 2 * k:2 * k + 1], op=OP.subtract),
          waits=[consts_done] if k == 0 else None,
          inc="DV" if k == 7 else None)
    lut_done = tok("DV")

    xld_free = [None, None]
    mm_tok = [None, None]
    for e in range(NE):
        bf = e % 2
        ld = DMA("scalar", "X%d" % bf, lambda sc, e=e, bf=bf: sc.dma_start(
            out=xld[:, bf, :], in_=xT[e * 128:(e + 1) * 128, :]),
            waits=[xld_free[bf]])
        cp = P("scalar", lambda sc2, e=e, bf=bf: sc2.copy(
            xn[:, e, :], xld[:, bf, :]), waits=[ld], inc="AC")
        sq = P("scalar", lambda sc, bf=bf: sc.activation(
            out=xsq[:, bf, :], in_=xld[:, bf, :], func=AF.Square),
            waits=[ld, mm_tok[bf]], inc="AC")
        xld_free[bf] = [cp, sq]
        for t in range(4):
            P("tensor", lambda te, e=e, t=t: te.matmul(
                pp[:, t, :], ones_sb, xn[:, e, t * 512:(t + 1) * 512],
                start=(e == 0), stop=(e == NE - 1)),
              waits=[cp] if t == 0 else None)
        for t in range(4):
            P("tensor", lambda te, e=e, t=t, bf=bf: te.matmul(
                pp[:, 4 + t, :], ones_sb, xsq[:, bf, t * 512:(t + 1) * 512],
                start=(e == 0), stop=(e == NE - 1)),
              waits=[sq] if t == 0 else None,
              inc="TE" if t == 3 else None)
        mm_tok[bf] = tok("TE")
    stats_done = tok("TE")

    for t in range(4):
        P("vector", lambda v, t=t: v.tensor_scalar(
            out=m_row[:, t * 512:(t + 1) * 512], in0=pp[:, t, :],
            scalar1=1.0 / 2048.0, scalar2=None, op0=OP.mult),
          waits=[stats_done] if t == 0 else None)
        P("vector", lambda v, t=t: v.tensor_scalar(
            out=q_row[:, t * 512:(t + 1) * 512], in0=pp[:, 4 + t, :],
            scalar1=1.0 / 2048.0, scalar2=None, op0=OP.mult))
    P("vector", lambda v: v.tensor_tensor(out=rb_row, in0=m_row, in1=m_row,
      op=OP.mult))
    P("vector", lambda v: v.tensor_tensor(out=q_row, in0=q_row, in1=rb_row,
      op=OP.subtract))
    vtok = P("vector", lambda v: v.tensor_scalar(out=q_row, in0=q_row,
      scalar1=EPS, scalar2=None, op0=OP.add), inc="DV")
    stok = P("scalar", lambda sc: sc.activation(out=rb_row, in_=q_row,
      func=AF.Sqrt), waits=[vtok], inc="AC")
    P("vector", lambda v: v.reciprocal(q_row, rb_row), waits=[stok])
    P("vector", lambda v: v.tensor_copy(rb_row, q_row))
    ln_row = P("vector", lambda v: v.tensor_tensor(out=mr_row, in0=m_row,
      in1=rb_row, op=OP.mult), inc="DV")

    for e in range(NE):
        for t4 in range(4):
            P("vector", lambda v, e=e, t4=t4: v.tensor_tensor(
                out=xn[:, e, t4 * 512:(t4 + 1) * 512],
                in0=xn[:, e, t4 * 512:(t4 + 1) * 512],
                in1=rb_row[:, t4 * 512:(t4 + 1) * 512], op=OP.mult),
              waits=[ln_row] if (e == 0 and t4 == 0) else None)
            P("vector", lambda v, e=e, t4=t4: v.tensor_tensor(
                out=xn[:, e, t4 * 512:(t4 + 1) * 512],
                in0=xn[:, e, t4 * 512:(t4 + 1) * 512],
                in1=mr_row[:, t4 * 512:(t4 + 1) * 512], op=OP.subtract),
              inc="DV" if (e == NE - 1 and t4 == 3) else None)
    xn_done = tok("DV")

    def proj(wbf, sems, sets, rhs, n_rhs, evac, first_mm_extra=None,
             set_free_init=None):
        ngr = ND // 2
        set_free = list(set_free_init) if set_free_init else [None, None]
        set_rdy = [None] * ngr
        uses = [0, 0]

        def load(g_):
            sl = g_ % 2
            for e in range(NE):
                DMA("sync", sems[sl], lambda s, g=g_, e=e, sl=sl:
                    s.dma_start_transpose(
                        sets[:, sl, e, :],
                        wbf[g * 256:(g + 1) * 256, e * 128:(e + 1) * 128]),
                    waits=[set_free[sl], wcast_done] if e == 0 else None)
            uses[sl] += 1
            set_rdy[g_] = tok(sems[sl])

        tokv = None
        load(0)
        for g_ in range(ngr):
            if g_ + 1 < ngr:
                load(g_ + 1)
            sl = g_ % 2
            for dt_l in range(2):
                dt = g_ * 2 + dt_l
                fw = [set_rdy[g_], tokv]
                if first_mm_extra and dt == 0:
                    fw += first_mm_extra
                for e in range(NE):
                    for c in range(n_rhs):
                        P("tensor", lambda te, dt_l=dt_l, e=e, c=c, sl=sl:
                          te.matmul(pp[:, c, :],
                                    sets[:, sl, e, dt_l * 128:(dt_l + 1) * 128],
                                    rhs(e, c), start=(e == 0),
                                    stop=(e == NE - 1)),
                          waits=fw if (e == 0 and c == 0) else None,
                          inc="TE" if (e == NE - 1 and c == n_rhs - 1) else None)
                tokv = evac(dt, n_rhs, tok("TE"))
            set_free[sl] = tokv
        return tokv

    def k_evac(dt, n_rhs, mm_done):
        for c in range(n_rhs):
            P("vector", lambda v, dt=dt, c=c: v.tensor_copy(
                k_sb[:, dt, c * 512:(c + 1) * 512], pp[:, c, :]),
              waits=[mm_done] if c == 0 else None,
              inc="DV" if c == n_rhs - 1 else None)
        return tok("DV")

    k_tok = proj(wk_bf, ("KS0", "KS1"), kv_sets,
                 lambda e, c: xn[:, e, c * 512:(c + 1) * 512], 4, k_evac,
                 first_mm_extra=[xn_done])

    vst = [0, 0]

    def v_evac(dt, n_rhs, mm_done):
        bfv = dt % 2
        for c in range(n_rhs):
            P("vector", lambda v, bfv=bfv, c=c: v.tensor_copy(
                vev[:, bfv, c * 512:(c + 1) * 512], pp[:, c, :]),
              waits=[mm_done, (C["VE%d" % bfv].s, 16 * vst[bfv])]
              if c == 0 else None,
              inc="DV" if c == n_rhs - 1 else None)
        ev = tok("DV")
        DMA("sync", "VE%d" % bfv, lambda s, dt=dt, bfv=bfv: s.dma_start(
            out=vfm[dt * 128:(dt + 1) * 128, :], in_=vev[:, bfv, :]),
            waits=[ev])
        vst[bfv] += 1
        return ev

    v_tok = proj(wv_bf, ("KS0", "KS1"), kv_sets,
                 lambda e, c: xn[:, e, c * 512:(c + 1) * 512], 4, v_evac,
                 set_free_init=[k_tok, k_tok])
    vfm_all = [(C["VE0"].s, 16 * vst[0]), (C["VE1"].s, 16 * vst[1])]

    for e in range(NE):
        P("gpsimd", lambda g, e=e: g.indirect_copy(
            out=xn_own[:, e, :].rearrange("p (i q) -> p i q", q=32),
            data=xn.rearrange("p a b -> p (a b)").rearrange(
                "p (i q) -> p i q", q=32),
            idxs=oc_sb[:, 2 * e:2 * e + 2], i_know_ap_gather_is_preferred=True),
            waits=[xn_done, consts_done, v_tok] if e == 0 else None,
            inc="GC" if e == NE - 1 else None)
    xcomp = tok("GC")



    def q_evac(dt, n_rhs, mm_done):
        for c in range(n_rhs):
            P("vector", lambda v, dt=dt, c=c: v.tensor_copy(
                q_sb[:, dt, c * 512:(c + 1) * 512], pp[:, c, :]),
              waits=[mm_done] if c == 0 else None,
              inc="DV" if c == n_rhs - 1 else None)
        return tok("DV")

    q_done = proj(wq_bf, ("QS0", "QS1"), qset_sets,
                  lambda e, c: xn_own[:, e, c * 512:(c + 1) * 512], 2, q_evac,
                  first_mm_extra=[xcomp, v_tok],
                  set_free_init=[xcomp, xcomp])

    if DBG or DBG_A:
        for e in range(NE):
            DMA("scalar", "AM", lambda s, e=e: s.dma_start(
                out=dbg_xn[e * 128:(e + 1) * 128, :], in_=xn[:, e, :]),
                waits=[xn_done, (C["AM"].s, C["AM"].n)])
            DMA("scalar", "AM", lambda s, e=e: s.dma_start(
                out=dbg_xo[e * 128:(e + 1) * 128, :], in_=xn_own[:, e, :]),
                waits=[xcomp, (C["AM"].s, C["AM"].n)])
            DMA("scalar", "AM", lambda s, e=e: s.dma_start(
                out=dbg_q[e * 128:(e + 1) * 128, :], in_=q_sb[:, e, :]),
                waits=[q_done, (C["AM"].s, C["AM"].n)])
            DMA("scalar", "AM", lambda s, e=e: s.dma_start(
                out=dbg_k[e * 128:(e + 1) * 128, :], in_=k_sb[:, e, :]),
                waits=[k_tok, (C["AM"].s, C["AM"].n)])
        am_cnt_dbg = 4 * NE
    vp_cnt = [0, 0]
    vpan_free = [None, None]
    prev_at = None
    ptb_free = [None, None]
    ptf_free = [None, None]
    am_cnt = [0]
    for j in range(NT):
        kt, kc = KT[j], KC[j]
        s0 = (S - 128) - 256 * j
        vrdy = [None] * kt

        def vload(t, j=j, vrdy=vrdy):
            sl = t % 2
            w = [vpan_free[sl]]
            if j == 0 and t < 2:
                w += vfm_all + [q_done]
            DMA("sync", "VP%d" % sl, lambda s, t=t, sl=sl: s.dma_start_transpose(
                vpan[:, sl, :], vfm[:, t * 128:(t + 1) * 128]), waits=w)
            vp_cnt[sl] += 1
            vrdy[t] = (C["VP%d" % sl].s, 16 * vp_cnt[sl])

        vload(0)
        if kt > 1:
            vload(1)
        for d in range(ND):
            for c in range(kc):
                w_c = min(512, kt * 128 - c * 512)
                fw = None
                if d == 0 and c == 0:
                    fw = [prev_at] if prev_at else [q_done, k_tok]
                P("tensor", lambda te, j=j, d=d, c=c, w_c=w_c: te.matmul(
                    pp[:, c, 0:w_c], q_sb[:, d, j * 128:(j + 1) * 128],
                    k_sb[:, d, c * 512:c * 512 + w_c],
                    start=(d == 0), stop=(d == ND - 1)),
                  waits=fw,
                  inc="TE" if (d == ND - 1 and c == kc - 1) else None)
        sc_done = tok("TE")
        for c in range(kc):
            w_c = min(512, kt * 128 - c * 512)
            wts = None
            if c == 0:
                wts = [sc_done, (C["AM"].s, C["AM"].n)]
            P("vector", lambda v, c=c, w_c=w_c, s0=s0: v.scalar_tensor_tensor(
                out=s_sb[:, c * 512:c * 512 + w_c], in0=pp[:, c, 0:w_c],
                scalar=SCALE, in1=band_sb[:, s0 + c * 512:s0 + c * 512 + w_c],
                op0=OP.mult, op1=OP.add),
              waits=wts, inc="DV" if c == kc - 1 else None)
        klen = kt * 128
        rm = P("vector", lambda v, klen=klen: v.reduce_max(
            nmax, s_sb[:, 0:klen], axis=AX.X, negate=True), inc="DV")
        ex = P("scalar", lambda sc, klen=klen: sc.activation(
            out=p_sb[:, 0:klen], in_=s_sb[:, 0:klen], func=AF.Exp,
            bias=nmax, scale=1.0, accum_out=rsum), waits=[rm], inc="AC")
        rc = P("vector", lambda v: v.reciprocal(rrec, rsum), waits=[ex],
               inc="DV")
        pn = P("vector", lambda v, klen=klen: v.tensor_scalar(
            out=p_sb[:, 0:klen], in0=p_sb[:, 0:klen], scalar1=rrec,
            scalar2=None, op0=OP.mult), waits=[rc], inc="DV")
        if DBG:
            DMA("scalar", "AM", lambda s, j=j, klen=klen: s.dma_start(
                out=dbg_s[j, :, 0:klen], in_=s_sb[:, 0:klen]),
                waits=[rm, (C["AM"].s, C["AM"].n)])
            DMA("scalar", "AM", lambda s, j=j, klen=klen: s.dma_start(
                out=dbg_p[j, :, 0:klen], in_=p_sb[:, 0:klen]),
                waits=[pn, (C["AM"].s, C["AM"].n)])
        pt_tok = []
        for t in range(kt):
            tr = P("tensor", lambda te, t=t: te.transpose(
                ptb[t % 2], p_sb[:, t * 128:(t + 1) * 128], idb_sb),
                waits=[pn if t == 0 else None, ptb_free[t % 2]], inc="TE")
            cpv = P("vector", lambda v, t=t: v.tensor_copy(
                pt_sb[:, t, :], ptb[t % 2]), waits=[tr], inc="DV")
            pt_tok.append(cpv)
            ptb_free[t % 2] = cpv
        for t in range(kt):
            sl = t % 2
            for dc in range(4):
                P("tensor", lambda te, t=t, dc=dc, sl=sl, kt=kt: te.matmul(
                    pp[:, dc, :], pt_sb[:, t, :],
                    vpan[:, sl, dc * 512:(dc + 1) * 512],
                    start=(t == 0), stop=(t == kt - 1)),
                  waits=[pt_tok[t], vrdy[t]] if dc == 0 else None,
                  inc="TE" if dc == 3 else None)
            vpan_free[sl] = tok("TE")
            if t + 2 < kt:
                vload(t + 2)
        av_done = tok("TE")
        for dc in range(4):
            P("vector", lambda v, dc=dc: v.tensor_copy(
                at_sb[:, dc * 512:(dc + 1) * 512], pp[:, dc, :]),
              waits=[av_done] if dc == 0 else None,
              inc="DV" if dc == 3 else None)
        at_done = tok("DV")
        prev_at = at_done
        for dc in range(ND):
            tr = P("tensor", lambda te, dc=dc: te.transpose(
                ptf[dc % 2], at_sb[:, dc * 128:(dc + 1) * 128], idf_sb),
                waits=[at_done if dc == 0 else None, ptf_free[dc % 2]],
                inc="TE")
            cpv = P("vector", lambda v, dc=dc: v.tensor_copy(
                af_sb[:, dc * 128:(dc + 1) * 128], ptf[dc % 2]),
                waits=[tr], inc="DV")
            ptf_free[dc % 2] = cpv
            DMA("scalar", "AM", lambda s, j=j, dc=dc: s.dma_start(
                out=attn_mid[j, dc * 128:(dc + 1) * 128, :],
                in_=af_sb.rearrange("p (a b) -> p a b", b=128)[:, dc, :]),
                waits=[cpv, (C["AM"].s, C["AM"].n - 16)])
            am_cnt[0] += 1
    attn_all = (C["AM"].s, C["AM"].n)

    pan_free = [attn_all, attn_all]
    slw_cnt = [0, 0]
    gp_cnt = [0, 0]
    slot_c = [0]

    def emit_gather(dst, idx_src, rows, cols):
        for r0 in range(0, rows, 128):
            for h0 in range(0, cols, 2816):
                hw = min(2816, cols - h0)
                sl = slot_c[0] % 2
                slot_c[0] += 1
                DMA("gpsimd", "GP%d" % sl, lambda g, r0=r0, h0=h0, hw=hw, sl=sl:
                    g.dma_start(out=gpan[:, sl, 0:hw],
                                in_=idx_src[r0:r0 + 128, h0:h0 + hw]),
                    waits=[pan_free[sl]])
                gp_cnt[sl] += 1
                ld = (C["GP%d" % sl].s, 16 * gp_cnt[sl])
                ct = None
                for c0 in range(0, hw, 2048):
                    cw = min(2048, hw - c0)
                    wts = None
                    if c0 == 0:
                        wts = [ld, lut_done,
                               (C["SL%d" % sl].s, 16 * slw_cnt[sl])]
                    P("vector", lambda v, sl=sl, c0=c0, cw=cw: v.tensor_scalar(
                        out=gm[:, 0, 0:cw], in0=gpan[:, sl, c0:c0 + cw],
                        scalar1=1, scalar2=None, op0=OP.bitwise_and), waits=wts)
                    P("vector", lambda v, cw=cw: v.tensor_copy(
                        gb0[:, 0:cw], gm[:, 0, 0:cw]))
                    for bmask, mi in ((2, 0), (4, 1), (8, 2)):
                        P("vector", lambda v, sl=sl, c0=c0, cw=cw, b=bmask,
                          mi=mi: v.tensor_scalar(
                            out=gm[:, mi, 0:cw], in0=gpan[:, sl, c0:c0 + cw],
                            scalar1=b, scalar2=None, op0=OP.bitwise_and))
                    for k in range(8):
                        P("vector", lambda v, cw=cw, k=k: v.tensor_scalar(
                            out=gt[:, k, 0:cw], in0=gb0[:, 0:cw],
                            scalar1=lutd_sb[:, k:k + 1],
                            scalar2=lut_sb[:, 2 * k:2 * k + 1],
                            op0=OP.mult, op1=OP.add))
                    for (a, b_) in ((0, 1), (2, 3), (4, 5), (6, 7)):
                        P("vector", lambda v, cw=cw, a=a, b_=b_:
                          v.copy_predicated(out=gt[:, a, 0:cw],
                                            mask=gm[:, 0, 0:cw],
                                            data=gt[:, b_, 0:cw]))
                    for (a, b_) in ((0, 2), (4, 6)):
                        P("vector", lambda v, cw=cw, a=a, b_=b_:
                          v.copy_predicated(out=gt[:, a, 0:cw],
                                            mask=gm[:, 1, 0:cw],
                                            data=gt[:, b_, 0:cw]))
                    P("vector", lambda v, cw=cw: v.copy_predicated(
                        out=gt[:, 0, 0:cw], mask=gm[:, 2, 0:cw],
                        data=gt[:, 4, 0:cw]))
                    ct = P("vector", lambda v, sl=sl, c0=c0, cw=cw:
                           v.tensor_copy(gslab[:, sl, c0:c0 + cw],
                                         gt[:, 0, 0:cw]), inc="DV")
                pan_free[sl] = ct
                DMA("sync", "SL%d" % sl, lambda s, r0=r0, h0=h0, hw=hw, sl=sl:
                    s.dma_start(out=dst[r0:r0 + 128, h0:h0 + hw],
                                in_=gslab[:, sl, 0:hw]), waits=[ct])
                slw_cnt[sl] += 1
        return [(C["SL0"].s, 16 * slw_cnt[0]), (C["SL1"].s, 16 * slw_cnt[1])]

    g_tok = emit_gather(w_g, gidx, II, E)
    u_tok = emit_gather(w_u, uidx, II, E)
    d_tok = emit_gather(w_d, didx, E, II)

    set_free = [None, None, None]
    fs_cnt = [0, 0, 0]
    h_tok = None
    ngrp = (NI + 1) // 2
    slot_i = 0
    for gi in range(ngrp):
        i0 = gi * 2
        ntl = min(2, NI - i0)
        r0 = gi * 256 if ntl == 2 else (NI * 128 - 256)
        il0 = 0 if ntl == 2 else 1
        slg = slot_i % 3
        slu = (slot_i + 1) % 3
        slot_i += 2
        for e in range(NE):
            DMA("sync", "FS%d" % slg, lambda s, r0=r0, e=e, slg=slg:
                s.dma_start_transpose(
                    ffn_sets[:, slg, e, :],
                    w_g[r0:r0 + 256, e * 128:(e + 1) * 128]),
                waits=([set_free[slg]] + (g_tok + d_tok if gi == 0 else []))
                if e == 0 else None)
        fs_cnt[slg] += 1
        grdy = (C["FS%d" % slg].s, 16 * NE * fs_cnt[slg])
        for e in range(NE):
            DMA("sync", "FS%d" % slu, lambda s, r0=r0, e=e, slu=slu:
                s.dma_start_transpose(
                    ffn_sets[:, slu, e, :],
                    w_u[r0:r0 + 256, e * 128:(e + 1) * 128]),
                waits=([set_free[slu]] + (u_tok if gi == 0 else []))
                if e == 0 else None)
        fs_cnt[slu] += 1
        urdy = (C["FS%d" % slu].s, 16 * NE * fs_cnt[slu])
        for il_ in range(ntl):
            il = il0 + il_
            it = i0 + il_
            for e in range(NE):
                for c in range(2):
                    P("tensor", lambda te, e=e, c=c, il=il, slg=slg: te.matmul(
                        pp[:, c, :], ffn_sets[:, slg, e, il * 128:(il + 1) * 128],
                        xn_own[:, e, c * 512:(c + 1) * 512],
                        start=(e == 0), stop=(e == NE - 1)),
                      waits=[grdy, h_tok] if (e == 0 and c == 0) else None,
                      inc="TE" if (e == NE - 1 and c == 1) else None)
            g_done = tok("TE")
            for e in range(NE):
                for c in range(2):
                    P("tensor", lambda te, e=e, c=c, il=il, slu=slu: te.matmul(
                        pp[:, 2 + c, :], ffn_sets[:, slu, e, il * 128:(il + 1) * 128],
                        xn_own[:, e, c * 512:(c + 1) * 512],
                        start=(e == 0), stop=(e == NE - 1)),
                      waits=[urdy] if (e == 0 and c == 0) else None,
                      inc="TE" if (e == NE - 1 and c == 1) else None)
            u_done = tok("TE")
            if DBG or DBG_F:
                for c in range(2):
                    P("vector", lambda v, c=c: v.tensor_copy(
                        arena[:, 124 * KB:132 * KB].bitcast(F32).rearrange("p (a b) -> p a b", b=1024)[:, 0, c * 512:(c + 1) * 512], pp[:, c, :]),
                      waits=[g_done] if c == 0 else None)
                    dgt = P("vector", lambda v, c=c: v.tensor_copy(
                        arena[:, 124 * KB:132 * KB].bitcast(F32).rearrange("p (a b) -> p a b", b=1024)[:, 1, c * 512:(c + 1) * 512], pp[:, 2 + c, :]),
                      waits=[u_done] if c == 0 else None,
                      inc="DV" if c == 1 else None)
                DMA("scalar", "AM", lambda s, it=it: s.dma_start(
                    out=dbg_g[it * 128:(it + 1) * 128, :], in_=arena[:, 124 * KB:132 * KB].bitcast(F32).rearrange("p (a b) -> p a b", b=1024)[:, 0, :]),
                    waits=[dgt, (C["AM"].s, C["AM"].n)])
                DMA("scalar", "AM", lambda s, it=it: s.dma_start(
                    out=dbg_u[it * 128:(it + 1) * 128, :], in_=arena[:, 124 * KB:132 * KB].bitcast(F32).rearrange("p (a b) -> p a b", b=1024)[:, 1, :]),
                    waits=[(C["AM"].s, C["AM"].n)])
            for c in range(2):
                sl_tok = P("scalar", lambda sc, c=c: sc.activation(
                    out=silu_sb[:, c, :], in_=pp[:, c, :], func=AF.Silu),
                    waits=[g_done, h_tok] if c == 0 else None, inc="AC")
                h_tok = P("vector", lambda v, it=it, c=c: v.tensor_tensor(
                    out=h_sb[:, it, c * 512:(c + 1) * 512],
                    in0=silu_sb[:, c, :], in1=pp[:, 2 + c, :], op=OP.mult),
                    waits=[sl_tok, u_done if c == 0 else None], inc="DV")
        set_free[slg] = h_tok
        set_free[slu] = h_tok
    gateup_done = h_tok

    if DBG or DBG_F or DBG_F2:
        for ic in range(NI):
            DMA("scalar", "AM", lambda s, ic=ic: s.dma_start(
                out=dbg_h[ic * 128:(ic + 1) * 128, :], in_=h_sb[:, ic, :]),
                waits=[gateup_done, (C["AM"].s, C["AM"].n)])

    dn_free = [None, None]
    ds_cnt = [0, 0]
    st_tok = None
    st_cnt = [0]
    xr_cnt = [0]
    ar_cnt = [0]
    bank_free = [None, None]
    for hg in range(ND // 2):
        sl = hg % 2
        for ic in range(NI):
            DMA("sync", "DS%d" % sl, lambda s, hg=hg, ic=ic, sl=sl:
                s.dma_start_transpose(
                    dn_sets[:, sl, ic, :],
                    w_d[hg * 256:(hg + 1) * 256, ic * 128:(ic + 1) * 128]),
                waits=([dn_free[sl], gateup_done] + (d_tok if hg == 0 else []))
                if ic == 0 else None)
        ds_cnt[sl] += 1
        drdy = (C["DS%d" % sl].s, 16 * NI * ds_cnt[sl])
        for hl in range(2):
            ht = hg * 2 + hl
            for th in range(2):
                DMA("scalar", "XR", lambda s, ht=ht, th=th: s.dma_start(
                    out=xres, in_=x_ownT[ht * 128:(ht + 1) * 128,
                                         th * 512:(th + 1) * 512]),
                    waits=[st_tok if st_tok else gateup_done])
                xr_cnt[0] += 1
                xr = (C["XR"].s, 16 * xr_cnt[0])
                for qb in range(4):
                    DMA("scalar", "AR", lambda s, ht=ht, th=th, qb=qb:
                        s.dma_start(out=ares[:, qb, :],
                                    in_=attn_mid[th * 4 + qb,
                                                 ht * 128:(ht + 1) * 128, :]),
                        waits=[attn_all, st_tok if st_tok else gateup_done]
                        if qb == 0 else None)
                ar_cnt[0] += 4
                art = (C["AR"].s, 16 * ar_cnt[0])
                for ic in range(NI):
                    P("tensor", lambda te, ic=ic, hl=hl, th=th, sl=sl: te.matmul(
                        pp[:, 4 + th, :],
                        dn_sets[:, sl, ic, hl * 128:(hl + 1) * 128],
                        h_sb[:, ic, th * 512:(th + 1) * 512],
                        start=(ic == 0), stop=(ic == NI - 1)),
                      waits=[drdy, bank_free[th], gateup_done]
                      if ic == 0 else None,
                      inc="TE" if ic == NI - 1 else None)
                dn_done = tok("TE")
                P("vector", lambda v, th=th: v.tensor_tensor(
                    out=ores[:, th, :], in0=pp[:, 4 + th, :], in1=xres,
                    op=OP.add), waits=[dn_done, xr])
                o_tok = P("vector", lambda v, th=th: v.tensor_tensor(
                    out=ores[:, th, :], in0=ores[:, th, :],
                    in1=ares.rearrange("p a b -> p (a b)"), op=OP.add),
                  waits=[art], inc="DV")
                bank_free[th] = o_tok
                DMA("scalar", "ST", lambda s, ht=ht, th=th: s.dma_start(
                    out=out[ht * 128:(ht + 1) * 128, th * 512:(th + 1) * 512],
                    in_=ores[:, th, :]),
                    waits=[o_tok, (C["ST"].s, 16 * st_cnt[0])])
                st_cnt[0] += 1
                st_tok = (C["ST"].s, 16 * st_cnt[0])
        dn_free[sl] = st_tok
    final_store = (C["ST"].s, 16 * st_cnt[0])

    if DBG_END:
        for ic in range(NI):
            DMA("scalar", "AM", lambda s, ic=ic: s.dma_start(
                out=dbg_h[ic * 128:(ic + 1) * 128, :], in_=h_sb[:, ic, :]),
                waits=[final_store, (C["AM"].s, C["AM"].n)])
    P("gpsimd", lambda g: g.wait_ge(final_store[0], final_store[1]))

    floors = {k: {} for k in ("sync", "scalar", "gpsimd", "vector", "tensor")}

    def emit_for(name, eng):
        fl = floors[name]
        for (e_, fn, waits, ctr) in plan:
            if e_ != name:
                continue
            for (s_, v_) in waits:
                key = id(s_)
                if fl.get(key, -1) < v_:
                    eng.wait_ge(s_, v_)
                    fl[key] = v_
            inst = fn(eng)
            if ctr is not None:
                inst.then_inc(ctr.s, ctr.step)

    with nc.Block() as block:
        @block.sync
        def _(e):
            emit_for("sync", e)

        @block.scalar
        def _(e):
            emit_for("scalar", e)

        @block.gpsimd
        def _(e):
            emit_for("gpsimd", e)

        @block.vector
        def _(e):
            emit_for("vector", e)

        @block.tensor
        def _(e):
            emit_for("tensor", e)

    return nc


_NC_CACHE = None


def _get_nc():
    global _NC_CACHE
    if _NC_CACHE is None:
        _NC_CACHE = _build()
    return _NC_CACHE


def kernel(L0_attn_x, L0_attn_Wk, L0_attn_Wq, L0_attn_Wv, L0_ffn_lut,
           L0_ffn_gate_idx, L0_ffn_up_idx, L0_ffn_down_idx):
    x = np.asarray(L0_attn_x, dtype=np.float32)
    wk_h = np.ascontiguousarray(np.asarray(L0_attn_Wk, dtype=np.float32))
    wq_h = np.ascontiguousarray(np.asarray(L0_attn_Wq, dtype=np.float32))
    wv_h = np.ascontiguousarray(np.asarray(L0_attn_Wv, dtype=np.float32))
    lut = np.asarray(L0_ffn_lut, dtype=np.float32).reshape(16)
    gi = np.ascontiguousarray(np.asarray(L0_ffn_gate_idx, dtype=np.int32))
    ui = np.ascontiguousarray(np.asarray(L0_ffn_up_idx, dtype=np.int32))
    di = np.ascontiguousarray(np.asarray(L0_ffn_down_idx, dtype=np.int32))

    lut128 = np.tile(lut.reshape(1, 16), (128, 1))
    ones128 = np.ones((128, 128), np.float32)
    id128 = np.eye(128, dtype=np.float32)

    in_maps = []
    for c in range(8):
        b, par = c // 2, c % 2
        xTb = np.ascontiguousarray(x[b].T)
        own = np.concatenate([np.arange((2 * j + par) * 128,
                                        (2 * j + par) * 128 + 128)
                              for j in range(NT)])
        x_ownTh = np.ascontiguousarray(xTb[:, own])
        cc = np.arange(2560)[None, :]
        ppn = np.arange(128)[:, None]
        band = np.where(cc - (S - 128) - par * 128 > ppn, -1.0e30,
                        0.0).astype(np.float32)
        ocols = np.zeros((128, 32), np.uint16)
        for e in range(NE):
            for p in range(128):
                for s_ in range(2):
                    i_ = s_ * 16 + (p % 16)
                    jt, qq = i_ // 4, i_ % 4
                    ocols[p, 2 * e + s_] = (e * 2048 + par * 128 +
                                            jt * 256 + qq * 32)
        in_maps.append({
            "xT": xTb, "x_ownT": x_ownTh, "wq": wq_h, "wk": wk_h, "wv": wv_h,
            "gidx": gi, "uidx": ui, "didx": di, "lut128": lut128,
            "ones_in": ones128, "id_in": id128, "band_in": band,
            "owncols": ocols,
        })

    nc = _get_nc()
    res = run_bass_kernel_spmd(nc, in_maps, core_ids=list(range(8)))

    outp = np.empty((B, S, E), np.float32)
    for c in range(8):
        b, par = c // 2, c % 2
        o = np.asarray(res.results[c]["out"])
        for j in range(NT):
            outp[b, (2 * j + par) * 128:(2 * j + par + 1) * 128, :] = \
                o[:, j * 128:(j + 1) * 128].T
    return outp
